# version 02 (final)
# baseline (speedup 1.0000x reference)
import contextlib
import numpy as np

import concourse.bass as bass
import concourse.mybir as mybir
from concourse.bass_utils import run_bass_kernel_spmd

F32 = mybir.dt.float32
BF16 = mybir.dt.bfloat16
I32 = mybir.dt.int32
AF = mybir.ActivationFunctionType
ALU = mybir.AluOpType

SLOTS = 8192
OWN0 = 4096
DM = 1024
DI = 2048
NS = 16
DTR = 64
DH = 4096
TCF = 256
TC = 512
NKD = DM // 128
NKH = DH // 128
NKI = DI // 128
FOLD_COLS = 1040
MAGIC = 0x5F3759DF
LNEPS = 1e-5



def _stage_diag(w, s, transposed):
    f = np.arange(DH)
    sig = 1 << s
    bit = (f >> s) & 1
    pair = (f >> (s + 1)) * sig + (f & (sig - 1))
    ws = w[s]
    d0 = np.where(bit, ws[pair, 1, 1], ws[pair, 0, 0])
    if transposed:
        d1 = np.where(bit, ws[pair, 0, 1], ws[pair, 1, 0])
    else:
        d1 = np.where(bit, ws[pair, 1, 0], ws[pair, 0, 1])
    return d0.astype(np.float32), d1.astype(np.float32)


def _gather(w, order, transposed, nk):
    out = np.zeros((2, nk, 128, 32), np.float32)
    for k, s in enumerate(order):
        d0, d1 = _stage_diag(w, s, transposed)
        out[0, k] = d0.reshape(32, 128).T
        out[1, k] = d1.reshape(32, 128).T
    return np.ascontiguousarray(out.transpose(2, 0, 1, 3).reshape(128, 2 * nk * 32))


def _perm_mats():
    out = np.zeros((128, 7 * 128), np.float32)
    for k in range(7):
        sig = 1 << k
        p = np.arange(128)
        out[p, k * 128 + (p ^ sig)] = 1.0
    return out


def _cols(v, n):
    return np.ascontiguousarray(np.asarray(v, np.float32).reshape(n, 128).T)


def host_inputs(inp, b, s):
    x = np.asarray(inp["L0_attn_hidden_states"])
    d = {}
    if s == 0:
        xt = np.zeros((DM, SLOTS), np.float32)
        xt[:, OWN0:] = x[b, :OWN0, :].T
        mask = np.zeros((1, SLOTS), np.float32)
        mask[0, OWN0:] = 1.0
    else:
        xt = np.ascontiguousarray(x[b].T)
        mask = np.ones((1, SLOTS), np.float32)
    d["xT"] = xt
    d["mask"] = mask

    wexp = np.asarray(inp["L0_ffn_W_exp"], np.float32)
    wex = np.zeros((DH, FOLD_COLS), np.float32)
    wex[:, :DM] = wexp.T
    wex[:, DM] = np.asarray(inp["L0_ffn_b_exp"], np.float32)
    d["wexpT"] = wex
    d["wcon"] = np.ascontiguousarray(np.asarray(inp["L0_ffn_W_con"], np.float32))
    d["inwT"] = np.ascontiguousarray(np.asarray(inp["L0_attn_in_proj_w"], np.float32).T)
    d["outwT"] = np.ascontiguousarray(np.asarray(inp["L0_attn_out_proj_w"], np.float32).T)
    d["xpwT"] = np.ascontiguousarray(np.asarray(inp["L0_attn_x_proj_w"], np.float32).T)
    d["dtwT"] = np.ascontiguousarray(np.asarray(inp["L0_attn_dt_proj_w"], np.float32).T)
    d["convw"] = np.concatenate(
        [_cols(np.asarray(inp["L0_attn_conv_w"])[:, k], NKI) for k in range(4)], axis=1)
    d["convb"] = _cols(inp["L0_attn_conv_b"], NKI)
    d["dtb"] = _cols(inp["L0_attn_dt_proj_b"], NKI)
    d["Dv"] = _cols(inp["L0_attn_D"], NKI)
    alog = np.asarray(inp["L0_attn_A_log"], np.float32)
    at = np.zeros((128, NKI * NS), np.float32)
    for dc in range(NKI):
        at[:, dc * NS:(dc + 1) * NS] = alog[dc * 128:(dc + 1) * 128, :]
    d["alog"] = at
    d["upbias"] = _cols(inp["L0_ffn_up_bias"], NKH)
    d["bcon"] = _cols(inp["L0_ffn_b_con"], NKD)
    d["dwnb"] = _cols(inp["L0_ffn_down_bias"], NKD)
    upw = np.asarray(inp["L0_ffn_up_weights"], np.float32)
    dnw = np.asarray(inp["L0_ffn_down_weights"], np.float32)
    d["wdAu"] = _gather(upw, [6, 5, 4, 3, 2, 1, 0], True, 7)
    d["wdAd"] = _gather(dnw, [0, 1, 2, 3, 4, 5, 6], False, 7)
    d["wdBu"] = _gather(upw, [7, 8, 9, 10, 11], False, 5)
    d["wdBd"] = _gather(dnw, [11, 10, 9, 8, 7], True, 5)
    d["perms"] = _perm_mats()
    d["istack"] = np.tile(np.eye(128, dtype=np.float32), (1, 32))
    d["idt"] = np.eye(128, dtype=np.float32)
    return d



def _emit_tail(block, SG):
    @block.gpsimd
    def _(gpsimd):
        gpsimd.wait_ge(SG.h, SG.n)


def _finish(es, nc):
    es.close()
    return nc


class Ctr:
    def __init__(self, es, nc, name):
        self.h = es.enter_context(nc.semaphore(name))
        self.n = 0

    def inc(self, instr, k=1):
        instr.then_inc(self.h, k)
        self.n += k
        return self.n


def build(tap=None, stop_after=None, nwin=None):
    nc = bass.Bass("TRN2", target_bir_lowering=False, num_devices=8, detect_race_conditions=False)

    def dram(name, shape, dtype):
        k = "ExternalOutput" if (tap == name or tap == "ALL") else "Internal"
        return nc.dram_tensor(name, shape, dtype, kind=k).ap()

    ein = lambda name, shape, dt=F32: nc.dram_tensor(name, shape, dt, kind="ExternalInput").ap()
    xT = ein("xT", [DM, SLOTS])
    maskd = ein("mask", [1, SLOTS])
    wexpT = ein("wexpT", [DH, FOLD_COLS])
    wcon = ein("wcon", [DH, DM])
    inwT = ein("inwT", [DM, DH])
    outwT = ein("outwT", [DI, DM])
    xpwT = ein("xpwT", [DI, 96])
    dtwT = ein("dtwT", [DTR, DI])
    convw_i = ein("convw", [128, 64])
    convb_i = ein("convb", [128, 16])
    dtb_i = ein("dtb", [128, 16])
    Dv_i = ein("Dv", [128, 16])
    alog_i = ein("alog", [128, 256])
    upb_i = ein("upbias", [128, 32])
    bcon_i = ein("bcon", [128, 8])
    dwn_i = ein("dwnb", [128, 8])
    wdAu_i = ein("wdAu", [128, 2 * 7 * 32])
    wdAd_i = ein("wdAd", [128, 2 * 7 * 32])
    wdBu_i = ein("wdBu", [128, 2 * 5 * 32])
    wdBd_i = ein("wdBd", [128, 2 * 5 * 32])
    perms_i = ein("perms", [128, 7 * 128])
    istack_i = ein("istack", [128, 4096])
    idt_i = ein("idt", [128, 128])

    out_d = nc.dram_tensor("out", [DM, OWN0], F32, kind="ExternalOutput").ap()
    w2_d = dram("w2_d", [DM, DH], BF16)
    wc_d = dram("wc_d", [DH, DM], BF16)
    x1_d = dram("x1_d", [DM, SLOTS], F32)
    x2_d = dram("x2_d", [DM, OWN0], F32)
    xcv_d = dram("xcv_d", [DI, SLOTS], BF16)
    dtr_d = dram("dtr_d", [DI, SLOTS], BF16)
    z_d = dram("z_d", [DI, OWN0], BF16)
    bc_d = dram("bc_d", [32, SLOTS], BF16)
    y_d = dram("y_d", [DI, OWN0], BF16)
    dbg_d = dram("dbg_d", [128, 256], F32)

    with contextlib.ExitStack() as es:
        sbt = lambda name, shape, dtype: es.enter_context(nc.sbuf_tensor(name, shape, dtype))
        psA = es.enter_context(nc.psum_tensor("psA", [128, 512], F32))
        psB = es.enter_context(nc.psum_tensor("psB", [128, 512], F32))
        psC = es.enter_context(nc.psum_tensor("psC", [128, 512], F32))
        psD = es.enter_context(nc.psum_tensor("psD", [128, 512], F32))
        psSx = es.enter_context(nc.psum_tensor("psSx", [128, 512], F32))
        psSq = es.enter_context(nc.psum_tensor("psSq", [128, 512], F32))
        psM = es.enter_context(nc.psum_tensor("psM", [128, 512], F32))
        psI = es.enter_context(nc.psum_tensor("psI", [128, 512], F32))

        onesf = sbt("onesf", [128, 128], F32)
        ones16 = sbt("ones16", [128, 128], BF16)
        idt16 = sbt("idt16", [128, 128], BF16)
        idtf = sbt("idtf", [128, 128], F32)
        convw = sbt("convw_s", [128, 64], F32)
        convb = sbt("convb_s", [128, 16], F32)
        dtb = sbt("dtb_s", [128, 16], F32)
        Dv = sbt("Dv_s", [128, 16], F32)
        Aneg = sbt("Aneg", [128, 256], F32)
        Apos = sbt("Apos", [128, 256], F32)
        bp = sbt("bp", [128, 32], F32)
        bp2 = sbt("bp2", [128, 32], F32)
        bfin = sbt("bfin", [128, 8], F32)
        dwnt = sbt("dwnt", [128, 8], F32)
        wdA = sbt("wdA", [128, 2 * 7 * 32], F32)
        wdB = sbt("wdB", [128, 2 * 5 * 32], F32)
        stt = sbt("stt", [128, 256], F32)
        scarry = sbt("scarry", [128, 16], F32)
        sendt = sbt("sendt", [128, 16], F32)
        dstt = sbt("dstt", [128, 16], F32)
        wst = sbt("wst", [128, 256], F32)
        darg = sbt("darg", [128, 256], F32)
        stats = sbt("stats", [1, 2048], F32)
        magict = sbt("magict", [1, 512], I32)
        junk16 = sbt("junk16", [128, 512], BF16)
        zerot16 = sbt("zerot16", [128, 512], BF16)
        tail = sbt("tail", [128, 16 * 3], F32)

        AR_COLS = 46080
        arena = sbt("arena", [128, AR_COLS], F32)

        class Alloc:
            def __init__(self):
                self.off = 0

            def f32(self, n):
                n = (n + 1) // 2 * 2
                assert self.off + n <= AR_COLS, (self.off, n)
                v = arena[:, self.off:self.off + n]
                self.off += n
                return v

            def bf16(self, n):
                return self.f32((n + 1) // 2).bitcast(BF16)

        ctr = {}
        LD = Ctr(es, nc, "LD")
        SG = Ctr(es, nc, "SG")
        SS = Ctr(es, nc, "SS")
        V = Ctr(es, nc, "V")
        S = Ctr(es, nc, "S")
        T = Ctr(es, nc, "T")
        block = es.enter_context(nc.Block())

        def join_all(eng):
            eng.wait_ge(V.h, V.n)
            eng.wait_ge(S.h, S.n)
            eng.wait_ge(T.h, T.n)
            eng.wait_ge(SG.h, SG.n)
            eng.wait_ge(SS.h, SS.n)
            eng.wait_ge(LD.h, LD.n)

        @block.sync
        def _(sync):
            sync.wait_ge(LD.h, LD.n)
            LD.inc(sync.dma_start(out=convw[:], in_=convw_i), 16)
            LD.inc(sync.dma_start(out=convb[:], in_=convb_i), 16)
            LD.inc(sync.dma_start(out=dtb[:], in_=dtb_i), 16)
            LD.inc(sync.dma_start(out=Dv[:], in_=Dv_i), 16)
            LD.inc(sync.dma_start(out=idtf[:], in_=idt_i), 16)
            LD.inc(sync.dma_start(out=Aneg[:], in_=alog_i), 16)
            LD.inc(sync.dma_start(out=bfin[:], in_=bcon_i), 16)
            LD.inc(sync.dma_start(out=dwnt[:], in_=dwn_i), 16)
            LD.inc(sync.dma_start(out=wdA[:], in_=wdAu_i), 16)
            LD.inc(sync.dma_start(out=wdB[:], in_=wdBu_i), 16)
        ld0 = LD.n

        @block.vector
        def _(vector):
            vector.wait_ge(LD.h, ld0)
            vector.memset(onesf[:], 1.0)
            vector.memset(ones16[:], 1.0)
            vector.memset(magict[:], MAGIC)
            vector.memset(junk16[:], 0.0)
            vector.memset(zerot16[:], 0.0)
            vector.memset(stt[:], 0.0)
            vector.memset(scarry[:], 0.0)
            vector.memset(tail[:], 0.0)
            vector.tensor_copy(out=idt16[:], in_=idtf[:])
            vector.tensor_add(out=bfin[:], in0=bfin[:], in1=dwnt[:])
            vector.tensor_copy(out=junk16[:, 0:8], in_=zerot16[:, 0:8])
            vector.tensor_scalar(out=bfin[:], in0=bfin[:], scalar1=0.5, scalar2=None,
                                 op0=ALU.mult)
            V.inc(vector.tensor_copy(out=Apos[:], in_=Aneg[:]))
        v0 = V.n

        @block.scalar
        def _(scalar):
            scalar.wait_ge(V.h, v0)
            S.inc(scalar.activation(out=Apos[:], in_=Apos[:], func=AF.Exp))
        s0 = S.n

        @block.vector
        def _(vector):
            vector.wait_ge(S.h, s0)
            V.inc(vector.tensor_scalar(out=Aneg[:], in0=Apos[:], scalar1=-1.0,
                                       scalar2=None, op0=ALU.mult))

        al = Alloc()
        perms_sb = al.f32(7 * 128)
        mstk = al.f32(4096)
        pmb = al.f32(4096)
        xt_f = al.f32(32 * 512)
        sp1f = al.f32(512)
        sp2f = al.f32(512)
        stg16 = al.bf16(512 * 32)

        def xf(j):
            return xt_f[:, j * 512:(j + 1) * 512]

        @block.sync
        def _(sync):
            sync.wait_ge(LD.h, LD.n)
            LD.inc(sync.dma_start(out=perms_sb, in_=perms_i), 16)
            LD.inc(sync.dma_start(out=mstk, in_=istack_i), 16)
        ld_m = LD.n

        @block.vector
        def _(vector):
            vector.wait_ge(LD.h, ld_m)
        @block.scalar
        def _(scalar):
            scalar.wait_ge(LD.h, ld_m)

        def emit_mbuild(order):
            for k, s_stage in enumerate(order):
                @block.tensor
                def _(tensor, s_stage=s_stage):
                    tensor.wait_ge(V.h, V.n)
                    tensor.wait_ge(S.h, S.n)
                    for ch in range(8):
                        p = psA if ch % 2 == 0 else psB
                        T.inc(tensor.matmul(out=p[:],
                                            lhsT=perms_sb[:, s_stage * 128:(s_stage + 1) * 128],
                                            rhs=mstk[:, ch * 512:(ch + 1) * 512],
                                            start=True, stop=True))
                t_mm = T.n
                @block.scalar
                def _(scalar, t_mm=t_mm):
                    scalar.wait_ge(T.h, t_mm)
                    for ch in range(8):
                        p = psA if ch % 2 == 0 else psB
                        S.inc(scalar.copy(out=pmb[:, ch * 512:(ch + 1) * 512], in_=p[:]))
                @block.scalar
                def _(scalar, k=k):
                    for j in range(32):
                        c0 = (0 * 7 + k) * 32 + j
                        c1 = (1 * 7 + k) * 32 + j
                        S.inc(scalar.activation(out=pmb[:, j * 128:(j + 1) * 128],
                                                in_=pmb[:, j * 128:(j + 1) * 128],
                                                func=AF.Identity, scale=wdA[:, c1:c1 + 1]))
                        S.inc(scalar.activation(out=mstk[:, j * 128:(j + 1) * 128],
                                                in_=mstk[:, j * 128:(j + 1) * 128],
                                                func=AF.Identity, scale=wdA[:, c0:c0 + 1]))
                s_sc = S.n
                @block.vector
                def _(vector, s_sc=s_sc):
                    vector.wait_ge(S.h, s_sc)
                    V.inc(vector.tensor_add(out=mstk, in0=mstk, in1=pmb))

        def emit_bstages(ncols, which):
            for k in range(5):
                sig_t = (1 << k) if which == "up" else (1 << (4 - k))
                done = set()
                for j in range(32):
                    jj = j ^ sig_t
                    if j in done:
                        continue
                    done.add(j)
                    done.add(jj)
                    a, b = j, jj
                    c0a = (0 * 5 + k) * 32 + a
                    c1a = (1 * 5 + k) * 32 + a
                    c0b = (0 * 5 + k) * 32 + b
                    c1b = (1 * 5 + k) * 32 + b
                    @block.scalar
                    def _(scalar, a=a, b=b, c1a=c1a, c1b=c1b):
                        scalar.wait_ge(V.h, V.n)
                        S.inc(scalar.activation(out=sp1f[:, 0:ncols], in_=xf(b)[:, 0:ncols],
                                                func=AF.Identity, scale=wdB[:, c1a:c1a + 1]))
                        S.inc(scalar.activation(out=sp2f[:, 0:ncols], in_=xf(a)[:, 0:ncols],
                                                func=AF.Identity, scale=wdB[:, c1b:c1b + 1]))
                    s_now = S.n
                    @block.vector
                    def _(vector, a=a, b=b, c0a=c0a, c0b=c0b, s_now=s_now):
                        vector.wait_ge(S.h, s_now)
                        V.inc(vector.scalar_tensor_tensor(out=xf(a)[:, 0:ncols],
                                                          in0=xf(a)[:, 0:ncols],
                                                          scalar=wdB[:, c0a:c0a + 1],
                                                          in1=sp1f[:, 0:ncols],
                                                          op0=ALU.mult, op1=ALU.add))
                        V.inc(vector.scalar_tensor_tensor(out=xf(b)[:, 0:ncols],
                                                          in0=xf(b)[:, 0:ncols],
                                                          scalar=wdB[:, c0b:c0b + 1],
                                                          in1=sp2f[:, 0:ncols],
                                                          op0=ALU.mult, op1=ALU.add))

        emit_mbuild([6, 5, 4, 3, 2, 1, 0])
        for cc0 in range(0, FOLD_COLS, 512):
            ncc = min(512, FOLD_COLS - cc0)
            @block.sync
            def _(sync, cc0=cc0, ncc=ncc):
                join_all(sync)
                for j in range(32):
                    LD.inc(sync.dma_start(out=xf(j)[:, 0:ncc],
                                          in_=wexpT[j * 128:(j + 1) * 128, cc0:cc0 + ncc]), 16)
            ld_c = LD.n
            @block.tensor
            def _(tensor):
                tensor.wait_ge(LD.h, ld_c)
            for j in range(32):
                p = psA if j % 2 == 0 else psB
                @block.tensor
                def _(tensor, j=j, p=p, ncc=ncc):
                    if j >= 2:
                        tensor.wait_ge(S.h, ctr[("fev", j - 2)])
                    T.inc(tensor.matmul(out=p[:, 0:ncc], lhsT=mstk[:, j * 128:(j + 1) * 128],
                                        rhs=xf(j)[:, 0:ncc], start=True, stop=True))
                t_j = T.n
                @block.scalar
                def _(scalar, j=j, p=p, ncc=ncc, t_j=t_j):
                    scalar.wait_ge(T.h, t_j)
                    S.inc(scalar.copy(out=xf(j)[:, 0:ncc], in_=p[:, 0:ncc]))
                    ctr[("fev", j)] = S.n
            s_a = S.n
            @block.vector
            def _(vector, s_a=s_a):
                vector.wait_ge(S.h, s_a)
            emit_bstages(ncc, "up")
            if cc0 <= DM < cc0 + ncc:
                @block.vector
                def _(vector, cc0=cc0):
                    for j in range(32):
                        V.inc(vector.tensor_copy(out=bp[:, j:j + 1],
                                                 in_=xf(j)[:, DM - cc0:DM - cc0 + 1]))
            if cc0 < DM:
                nrc = min(512, DM - cc0) // 128
                v_b = V.n
                @block.tensor
                def _(tensor, v_b=v_b):
                    tensor.wait_ge(V.h, v_b)
                for j in range(32):
                    for rc in range(nrc):
                        i = j * nrc + rc
                        p = psA if i % 2 == 0 else psB
                        @block.tensor
                        def _(tensor, j=j, rc=rc, i=i, p=p):
                            if i >= 2:
                                tensor.wait_ge(S.h, ctr[("ftr", i - 2)])
                            T.inc(tensor.transpose(out=p[:, 0:128],
                                                   in_=xf(j)[:, rc * 128:(rc + 1) * 128],
                                                   identity=idtf[:]))
                        t_i = T.n
                        @block.scalar
                        def _(scalar, i=i, p=p, t_i=t_i):
                            scalar.wait_ge(T.h, t_i)
                            S.inc(scalar.copy(out=stg16[:, i * 128:(i + 1) * 128],
                                              in_=p[:, 0:128]))
                            ctr[("ftr", i)] = S.n
                s_tr = S.n
                @block.gpsimd
                def _(gpsimd, cc0=cc0, nrc=nrc, s_tr=s_tr):
                    gpsimd.wait_ge(SG.h, SG.n)
                    gpsimd.wait_ge(S.h, s_tr)
                    for j in range(32):
                        for rc in range(nrc):
                            i = j * nrc + rc
                            SG.inc(gpsimd.dma_start(
                                out=w2_d[cc0 + rc * 128:cc0 + (rc + 1) * 128,
                                         j * 128:(j + 1) * 128],
                                in_=stg16[:, i * 128:(i + 1) * 128]), 16)

        @block.sync
        def _(sync):
            join_all(sync)
            LD.inc(sync.dma_start(out=bp2[:], in_=upb_i), 16)
        ld_b = LD.n
        @block.vector
        def _(vector, ld_b=ld_b):
            vector.wait_ge(LD.h, ld_b)
            V.inc(vector.tensor_add(out=bp[:], in0=bp[:], in1=bp2[:]))
            vector.tensor_copy(out=junk16[:, 0:8], in_=zerot16[:, 0:8])
            V.inc(vector.tensor_scalar(out=bp2[:], in0=bp[:], scalar1=0.70710678,
                                       scalar2=None, op0=ALU.mult))

        @block.sync
        def _(sync):
            join_all(sync)
            LD.inc(sync.dma_start(out=wdA[:], in_=wdAd_i), 16)
            LD.inc(sync.dma_start(out=wdB[:], in_=wdBd_i), 16)
            LD.inc(sync.dma_start(out=mstk, in_=istack_i), 16)
        ld_d = LD.n
        @block.vector
        def _(vector, ld_d=ld_d):
            vector.wait_ge(LD.h, ld_d)
        @block.scalar
        def _(scalar, ld_d=ld_d):
            scalar.wait_ge(LD.h, ld_d)
        emit_mbuild([0, 1, 2, 3, 4, 5, 6])
        for cc0 in range(0, DM, 512):
            @block.sync
            def _(sync, cc0=cc0):
                join_all(sync)
                for j in range(32):
                    LD.inc(sync.dma_start(out=xf(j),
                                          in_=wcon[j * 128:(j + 1) * 128, cc0:cc0 + 512]), 16)
            ld_c = LD.n
            @block.vector
            def _(vector):
                vector.wait_ge(LD.h, ld_c)
            @block.scalar
            def _(scalar):
                scalar.wait_ge(LD.h, ld_c)
            emit_bstages(512, "dn")
            v_b = V.n
            @block.tensor
            def _(tensor, v_b=v_b):
                tensor.wait_ge(V.h, v_b)
            for j in range(32):
                p = psA if j % 2 == 0 else psB
                @block.tensor
                def _(tensor, j=j, p=p):
                    if j >= 2:
                        tensor.wait_ge(S.h, ctr[("fdv", j - 2)])
                    T.inc(tensor.matmul(out=p[:], lhsT=mstk[:, j * 128:(j + 1) * 128],
                                        rhs=xf(j), start=True, stop=True))
                t_j = T.n
                @block.scalar
                def _(scalar, j=j, p=p, t_j=t_j):
                    scalar.wait_ge(T.h, t_j)
                    S.inc(scalar.activation(out=stg16[:, j * 512:(j + 1) * 512], in_=p[:],
                                            func=AF.Identity, scale=0.25))
                    ctr[("fdv", j)] = S.n
            s_dn = S.n
            @block.gpsimd
            def _(gpsimd, cc0=cc0, s_dn=s_dn):
                gpsimd.wait_ge(SG.h, SG.n)
                gpsimd.wait_ge(S.h, s_dn)
                for j in range(32):
                    SG.inc(gpsimd.dma_start(out=wc_d[j * 128:(j + 1) * 128, cc0:cc0 + 512],
                                            in_=stg16[:, j * 512:(j + 1) * 512]), 16)

        def emit_ln(xtiles, xn16, ncols, cast16, sq16, lntmp, mask_row=None):
            @block.vector
            def _(vector):
                for k in range(8):
                    V.inc(vector.tensor_copy(out=cast16[k], in_=xtiles[k]))
                for k in range(8):
                    V.inc(vector.tensor_mul(out=sq16[k], in0=cast16[k], in1=cast16[k]))
            v_cast = V.n
            @block.tensor
            def _(tensor, v_cast=v_cast):
                tensor.wait_ge(V.h, v_cast)
                for k in range(8):
                    T.inc(tensor.matmul(out=psSx[0:1, 0:ncols], lhsT=ones16[:, 0:1],
                                        rhs=cast16[k], start=(k == 0), stop=(k == 7)))
                for k in range(8):
                    T.inc(tensor.matmul(out=psSq[0:1, 0:ncols], lhsT=ones16[:, 0:1],
                                        rhs=sq16[k], start=(k == 0), stop=(k == 7)))
            t_st = T.n
            mrow = stats[0:1, 0:ncols]
            irow = stats[0:1, 512:512 + ncols]
            t1r = stats[0:1, 1024:1024 + ncols]
            t2r = stats[0:1, 1536:1536 + ncols]
            @block.vector
            def _(vector, t_st=t_st):
                vector.wait_ge(T.h, t_st)
                V.inc(vector.tensor_scalar(out=mrow, in0=psSx[0:1, 0:ncols],
                                           scalar1=1.0 / DM, scalar2=None, op0=ALU.mult))
                V.inc(vector.tensor_scalar(out=t1r, in0=psSq[0:1, 0:ncols],
                                           scalar1=1.0 / DM, scalar2=None, op0=ALU.mult))
                V.inc(vector.tensor_mul(out=t2r, in0=mrow, in1=mrow))
                V.inc(vector.tensor_sub(out=t1r, in0=t1r, in1=t2r))
                V.inc(vector.tensor_scalar(out=t1r, in0=t1r, scalar1=LNEPS,
                                           scalar2=None, op0=ALU.add))
                V.inc(vector.tensor_scalar(out=irow.bitcast(I32), in0=t1r.bitcast(I32),
                                           scalar1=1, scalar2=None,
                                           op0=ALU.logical_shift_right))
                V.inc(vector.tensor_sub(out=irow.bitcast(I32), in0=magict[0:1, 0:ncols],
                                        in1=irow.bitcast(I32)))
                for _i in range(4):
                    V.inc(vector.tensor_mul(out=t2r, in0=t1r, in1=irow))
                    V.inc(vector.tensor_mul(out=t2r, in0=t2r, in1=irow))
                    V.inc(vector.tensor_scalar(out=t2r, in0=t2r, scalar1=-0.5, scalar2=1.5,
                                               op0=ALU.mult, op1=ALU.add))
                    V.inc(vector.tensor_mul(out=irow, in0=irow, in1=t2r))
                if mask_row is not None:
                    V.inc(vector.tensor_mul(out=irow, in0=irow, in1=mask_row))
            v_stats = V.n
            @block.tensor
            def _(tensor, v_stats=v_stats):
                tensor.wait_ge(V.h, v_stats)
                T.inc(tensor.matmul(out=psM[:, 0:ncols], lhsT=onesf[0:1, 0:128],
                                    rhs=mrow, start=True, stop=True))
                T.inc(tensor.matmul(out=psI[:, 0:ncols], lhsT=onesf[0:1, 0:128],
                                    rhs=irow, start=True, stop=True))
            t_bc = T.n
            @block.vector
            def _(vector, t_bc=t_bc):
                vector.wait_ge(T.h, t_bc)
                for k in range(8):
                    V.inc(vector.tensor_sub(out=lntmp, in0=xtiles[k], in1=psM[:, 0:ncols]))
                    V.inc(vector.tensor_mul(out=xn16[k], in0=lntmp, in1=psI[:, 0:ncols]))

        def emit_ffn_chunk(w2v, wcv, xtiles, xn16, cast16, sq16, lntmp, g2,
                           hb2, eb2, outtiles, ncols):
            emit_ln(xtiles, xn16, ncols, cast16, sq16, lntmp, None)
            v_ln = V.n
            for m in range(NKH):
                pp = psA if m % 2 == 0 else psB
                hbuf = hb2[m % 2]
                ebuf = eb2[m % 2]
                @block.tensor
                def _(tensor, m=m, pp=pp):
                    if m == 0:
                        tensor.wait_ge(V.h, v_ln)
                    if m >= 2:
                        tensor.wait_ge(V.h, ctr[("g2", m - 2)])
                    for k in range(NKD):
                        T.inc(tensor.matmul(out=pp[:, 0:ncols],
                                            lhsT=w2v[:, k * DH + m * 128:k * DH + (m + 1) * 128],
                                            rhs=xn16[k], start=(k == 0), stop=(k == NKD - 1)))
                t_m = T.n
                @block.scalar
                def _(scalar, m=m, pp=pp, t_m=t_m, hbuf=hbuf, ebuf=ebuf):
                    scalar.wait_ge(T.h, t_m)
                    if m >= 2:
                        scalar.wait_ge(V.h, ctr[("g2", m - 2)])
                    S.inc(scalar.activation(out=hbuf, in_=pp[:, 0:ncols], func=AF.Identity,
                                            bias=bp[:, m:m + 1]))
                    S.inc(scalar.activation(out=ebuf, in_=pp[:, 0:ncols], func=AF.Erf,
                                            scale=0.70710678, bias=bp2[:, m:m + 1]))
                s_m = S.n
                @block.vector
                def _(vector, m=m, s_m=s_m, hbuf=hbuf, ebuf=ebuf):
                    vector.wait_ge(S.h, s_m)
                    V.inc(vector.scalar_tensor_tensor(out=g2[m], in0=ebuf, scalar=1.0,
                                                      in1=hbuf, op0=ALU.add, op1=ALU.mult))
                    ctr[("g2", m)] = V.n
            v_g2 = V.n
            for mo in range(NKD):
                pp = psA if mo % 2 == 0 else psB
                @block.tensor
                def _(tensor, mo=mo, pp=pp):
                    if mo == 0:
                        tensor.wait_ge(V.h, v_g2)
                    if mo >= 2:
                        tensor.wait_ge(V.h, ctr[("ffo", mo - 2)])
                    for k2 in range(NKH):
                        T.inc(tensor.matmul(out=pp[:, 0:ncols],
                                            lhsT=wcv[:, k2 * DM + mo * 128:k2 * DM + (mo + 1) * 128],
                                            rhs=g2[k2], start=(k2 == 0), stop=(k2 == NKH - 1)))
                t_m2 = T.n
                @block.vector
                def _(vector, mo=mo, pp=pp, t_m2=t_m2):
                    vector.wait_ge(T.h, t_m2)
                    V.inc(vector.scalar_tensor_tensor(out=outtiles[mo], in0=pp[:, 0:ncols],
                                                      scalar=bfin[:, mo:mo + 1],
                                                      in1=xtiles[mo],
                                                      op0=ALU.add, op1=ALU.add))
                    ctr[("ffo", mo)] = V.n

        if stop_after == 'fold':
            _emit_tail(block, SG)
            return _finish(es, nc)
        al = Alloc()
        w2v = al.bf16(NKD * DH)
        wcv = al.bf16(NKH * DM)
        a_x = [al.f32(TCF) for _ in range(8)]
        a_xn = [al.bf16(TCF) for _ in range(8)]
        a_c16 = [al.bf16(TCF) for _ in range(8)]
        a_sq = [al.bf16(TCF) for _ in range(8)]
        a_g2 = [al.bf16(TCF) for _ in range(NKH)]
        a_hb = [al.f32(TCF) for _ in range(2)]
        a_eb = [al.f32(TCF) for _ in range(2)]
        a_tmp = al.f32(TCF)
        a_o = [al.f32(TCF) for _ in range(8)]

        @block.sync
        def _(sync):
            join_all(sync)
            for k in range(NKD):
                LD.inc(sync.dma_start(out=w2v[:, k * DH:(k + 1) * DH],
                                      in_=w2_d[k * 128:(k + 1) * 128, :]), 16)
            for k in range(NKH):
                LD.inc(sync.dma_start(out=wcv[:, k * DM:(k + 1) * DM],
                                      in_=wc_d[k * 128:(k + 1) * 128, :]), 16)

        for c in range(SLOTS // TCF):
            t0 = c * TCF
            @block.sync
            def _(sync, t0=t0):
                join_all(sync)
                for k in range(8):
                    LD.inc(sync.dma_start(out=a_x[k],
                                          in_=xT[k * 128:(k + 1) * 128, t0:t0 + TCF]), 16)
            ld_c = LD.n
            @block.vector
            def _(vector):
                vector.wait_ge(LD.h, ld_c)
            @block.tensor
            def _(tensor):
                tensor.wait_ge(LD.h, ld_c)
            emit_ffn_chunk(w2v, wcv, a_x, a_xn, a_c16, a_sq, a_tmp, a_g2,
                           a_hb, a_eb, a_o, TCF)
            v_o = V.n
            @block.gpsimd
            def _(gpsimd, t0=t0, v_o=v_o):
                gpsimd.wait_ge(SG.h, SG.n)
                gpsimd.wait_ge(V.h, v_o)
                for k in range(8):
                    SG.inc(gpsimd.dma_start(out=x1_d[k * 128:(k + 1) * 128, t0:t0 + TCF],
                                            in_=a_o[k]), 16)

        if stop_after == 'A1':
            _emit_tail(block, SG)
            return _finish(es, nc)
        al = Alloc()
        wiv = al.bf16(NKD * DH)
        wxp = al.bf16(NKI * 96)
        wdt = al.bf16(DI)
        wstage = al.f32(4096)
        b_x1 = [al.f32(TC) for _ in range(8)]
        b_xn = [al.bf16(TC) for _ in range(8)]
        b_c16 = [al.bf16(TC) for _ in range(8)]
        b_sq = [al.bf16(TC) for _ in range(8)]
        b_tmp = al.f32(TC)
        b_xz = [al.bf16(TC + 4) for _ in range(NKI)]
        b_xc = [al.f32(TC) for _ in range(2)]
        b_sg = [al.f32(TC) for _ in range(2)]
        b_xcv = [al.bf16(TC) for _ in range(NKI)]
        b_dtlo = al.bf16(TC)
        b_bc = al.bf16(TC)
        b_mrow = al.f32(TC)
        b_z16 = [al.bf16(TC) for _ in range(2)]
        b_dt16 = [al.bf16(TC) for _ in range(2)]

        @block.sync
        def _(sync):
            join_all(sync)
        for k in range(NKD):
            @block.sync
            def _(sync, k=k):
                sync.wait_ge(LD.h, LD.n)
                if k >= 1:
                    sync.wait_ge(V.h, ctr[("wc", k - 1)])
                LD.inc(sync.dma_start(out=wstage, in_=inwT[k * 128:(k + 1) * 128, :]), 16)
            ld_k = LD.n
            @block.vector
            def _(vector, k=k, ld_k=ld_k):
                vector.wait_ge(LD.h, ld_k)
                V.inc(vector.tensor_copy(out=wiv[:, k * DH:(k + 1) * DH], in_=wstage))
                ctr[("wc", k)] = V.n
        for k in range(NKI):
            @block.sync
            def _(sync, k=k):
                sync.wait_ge(LD.h, LD.n)
                sync.wait_ge(V.h, ctr[("wc", NKD - 1)] if k == 0 else ctr[("wx", k - 1)])
                LD.inc(sync.dma_start(out=wstage[:, 0:96],
                                      in_=xpwT[k * 128:(k + 1) * 128, :]), 16)
            ld_k = LD.n
            @block.vector
            def _(vector, k=k, ld_k=ld_k):
                vector.wait_ge(LD.h, ld_k)
                V.inc(vector.tensor_copy(out=wxp[:, k * 96:(k + 1) * 96],
                                         in_=wstage[:, 0:96]))
                ctr[("wx", k)] = V.n
        @block.sync
        def _(sync):
            sync.wait_ge(LD.h, LD.n)
            sync.wait_ge(V.h, ctr[("wx", NKI - 1)])
            LD.inc(sync.dma_start(out=wstage[0:64, 0:DI], in_=dtwT), 16)
        ld_k = LD.n
        @block.vector
        def _(vector, ld_k=ld_k):
            vector.wait_ge(LD.h, ld_k)
            V.inc(vector.tensor_copy(out=wdt[0:64, :], in_=wstage[0:64, 0:DI]))

        for c in range(SLOTS // TC):
            t0 = c * TC
            own = t0 >= OWN0
            @block.sync
            def _(sync, t0=t0):
                join_all(sync)
                for k in range(8):
                    LD.inc(sync.dma_start(out=b_x1[k],
                                          in_=x1_d[k * 128:(k + 1) * 128, t0:t0 + TC]), 16)
                LD.inc(sync.dma_start(out=b_mrow[0:1, :], in_=maskd[0:1, t0:t0 + TC]), 16)
            ld_c = LD.n
            @block.vector
            def _(vector):
                vector.wait_ge(LD.h, ld_c)
                for dc in range(NKI):
                    vector.tensor_copy(out=b_xz[dc][:, 0:3], in_=tail[:, dc * 3:(dc + 1) * 3])
            @block.tensor
            def _(tensor):
                tensor.wait_ge(LD.h, ld_c)
            emit_ln(b_x1, b_xn, TC, b_c16, b_sq, b_tmp, b_mrow[0:1, :])
            v_ln = V.n
            mlist = list(range(NKI)) + (list(range(NKI, 2 * NKI)) if own else [])
            for mi, m in enumerate(mlist):
                pp = psA if mi % 2 == 0 else psB
                @block.tensor
                def _(tensor, m=m, mi=mi, pp=pp):
                    if mi == 0:
                        tensor.wait_ge(V.h, v_ln)
                    if mi >= 2:
                        tensor.wait_ge(S.h, ctr[("xze", mi - 2)])
                    for k in range(NKD):
                        T.inc(tensor.matmul(out=pp[:, 0:TC],
                                            lhsT=wiv[:, k * DH + m * 128:k * DH + (m + 1) * 128],
                                            rhs=b_xn[k], start=(k == 0), stop=(k == NKD - 1)))
                t_m = T.n
                @block.scalar
                def _(scalar, m=m, mi=mi, t_m=t_m, pp=pp):
                    scalar.wait_ge(T.h, t_m)
                    if m < NKI:
                        S.inc(scalar.copy(out=b_xz[m][:, 3:3 + TC], in_=pp[:, 0:TC]))
                    else:
                        zi = (m - NKI) % 2
                        if m - NKI >= 2:
                            scalar.wait_ge(SG.h, ctr[("zst", m - NKI - 2)])
                        S.inc(scalar.copy(out=b_z16[zi], in_=pp[:, 0:TC]))
                    ctr[("xze", mi)] = S.n
                if m >= NKI:
                    s_z = S.n
                    @block.gpsimd
                    def _(gpsimd, m=m, t0=t0, s_z=s_z):
                        gpsimd.wait_ge(SG.h, SG.n)
                        gpsimd.wait_ge(S.h, s_z)
                        dcz = m - NKI
                        SG.inc(gpsimd.dma_start(
                            out=z_d[dcz * 128:(dcz + 1) * 128, t0 - OWN0:t0 - OWN0 + TC],
                            in_=b_z16[dcz % 2]), 16)
                        ctr[("zst", m - NKI)] = SG.n
            s_xz = S.n
            for dc in range(NKI):
                xc = b_xc[dc % 2]
                sg = b_sg[dc % 2]
                @block.vector
                def _(vector, dc=dc, xc=xc):
                    if dc == 0:
                        vector.wait_ge(S.h, s_xz)
                    if dc >= 2:
                        vector.wait_ge(S.h, ctr[("sg", dc - 2)])
                    V.inc(vector.tensor_scalar(out=xc, in0=b_xz[dc][:, 0:TC],
                                               scalar1=convw[:, 0 * 16 + dc:0 * 16 + dc + 1],
                                               scalar2=convb[:, dc:dc + 1],
                                               op0=ALU.mult, op1=ALU.add))
                    for kk in range(1, 4):
                        V.inc(vector.scalar_tensor_tensor(out=xc, in0=b_xz[dc][:, kk:kk + TC],
                                                          scalar=convw[:, kk * 16 + dc:kk * 16 + dc + 1],
                                                          in1=xc, op0=ALU.mult, op1=ALU.add))
                    ctr[("cv", dc)] = V.n
                @block.scalar
                def _(scalar, dc=dc, xc=xc, sg=sg):
                    scalar.wait_ge(V.h, ctr[("cv", dc)])
                    S.inc(scalar.activation(out=sg, in_=xc, func=AF.Sigmoid))
                    ctr[("sg", dc)] = S.n
                @block.vector
                def _(vector, dc=dc, xc=xc, sg=sg):
                    vector.wait_ge(S.h, ctr[("sg", dc)])
                    V.inc(vector.tensor_mul(out=b_xcv[dc], in0=xc, in1=sg))
                    V.inc(vector.tensor_copy(out=tail[:, dc * 3:(dc + 1) * 3],
                                             in_=b_xz[dc][:, TC:TC + 3]))
                    ctr[("xcv", dc)] = V.n
                @block.gpsimd
                def _(gpsimd, dc=dc, t0=t0):
                    gpsimd.wait_ge(SG.h, SG.n)
                    gpsimd.wait_ge(V.h, ctr[("xcv", dc)])
                    SG.inc(gpsimd.dma_start(out=xcv_d[dc * 128:(dc + 1) * 128, t0:t0 + TC],
                                            in_=b_xcv[dc]), 16)
            v_cv = V.n
            @block.tensor
            def _(tensor, v_cv=v_cv):
                tensor.wait_ge(V.h, v_cv)
                for k in range(NKI):
                    T.inc(tensor.matmul(out=psC[0:96, 0:TC], lhsT=wxp[:, k * 96:(k + 1) * 96],
                                        rhs=b_xcv[k], start=(k == 0), stop=(k == NKI - 1)))
            t_xp = T.n
            @block.scalar
            def _(scalar, t_xp=t_xp):
                scalar.wait_ge(T.h, t_xp)
                S.inc(scalar.copy(out=b_dtlo[0:64, :], in_=psC[0:64, 0:TC]))
                S.inc(scalar.copy(out=b_bc[0:32, :], in_=psC[64:96, 0:TC]))
            s_xp = S.n
            @block.gpsimd
            def _(gpsimd, t0=t0, s_xp=s_xp):
                gpsimd.wait_ge(SG.h, SG.n)
                gpsimd.wait_ge(S.h, s_xp)
                SG.inc(gpsimd.dma_start(out=bc_d[:, t0:t0 + TC], in_=b_bc[0:32, :]), 16)
            for mi in range(NKI):
                pp = psA if mi % 2 == 0 else psB
                dst = b_dt16[mi % 2]
                @block.tensor
                def _(tensor, mi=mi, pp=pp):
                    if mi == 0:
                        tensor.wait_ge(S.h, s_xp)
                    if mi >= 2:
                        tensor.wait_ge(S.h, ctr[("dte", mi - 2)])
                    T.inc(tensor.matmul(out=pp[:, 0:TC],
                                        lhsT=wdt[0:64, mi * 128:(mi + 1) * 128],
                                        rhs=b_dtlo[0:64, :], start=True, stop=True))
                t_dt = T.n
                @block.scalar
                def _(scalar, mi=mi, pp=pp, t_dt=t_dt, dst=dst):
                    scalar.wait_ge(T.h, t_dt)
                    if mi >= 2:
                        scalar.wait_ge(SG.h, ctr[("dts", mi - 2)])
                    S.inc(scalar.activation(out=dst, in_=pp[:, 0:TC],
                                            func=AF.Identity, bias=dtb[:, mi:mi + 1]))
                    ctr[("dte", mi)] = S.n
                @block.gpsimd
                def _(gpsimd, mi=mi, t0=t0, dst=dst):
                    gpsimd.wait_ge(SG.h, SG.n)
                    gpsimd.wait_ge(S.h, ctr[("dte", mi)])
                    SG.inc(gpsimd.dma_start(out=dtr_d[mi * 128:(mi + 1) * 128, t0:t0 + TC],
                                            in_=dst), 16)
                    ctr[("dts", mi)] = SG.n

        if stop_after == 'A2':
            _emit_tail(block, SG)
            return _finish(es, nc)
        al = Alloc()
        s_dt = [al.bf16(TC) for _ in range(NKI)]
        s_xcv = [al.bf16(TC) for _ in range(NKI)]
        s_u = [al.bf16(TC) for _ in range(NKI)]
        s_Bb = al.bf16(NS * TC)
        s_Cb = al.bf16(NS * TC)
        s_maskB = al.f32(TC)
        s_scum = [al.f32(TC) for _ in range(NKI)]
        s_sp = al.f32(TC)
        s_dA = [al.f32(TC) for _ in range(2)]
        s_E = [al.f32(TC) for _ in range(2)]
        s_U = [al.bf16(TC) for _ in range(2)]
        s_H = [al.f32(TC) for _ in range(2)]
        s_P = [al.bf16(TC) for _ in range(2)]
        s_y16 = [al.bf16(TC) for _ in range(2)]
        s_dx = [al.bf16(TC) for _ in range(2)]


        for w in range(SLOTS // TC if nwin is None else nwin):
            t0 = w * TC
            own = t0 >= OWN0
            @block.sync
            def _(sync, t0=t0):
                join_all(sync)
                for dc in range(NKI):
                    LD.inc(sync.dma_start(out=s_dt[dc],
                                          in_=dtr_d[dc * 128:(dc + 1) * 128, t0:t0 + TC]), 16)
                    LD.inc(sync.dma_start(out=s_xcv[dc],
                                          in_=xcv_d[dc * 128:(dc + 1) * 128, t0:t0 + TC]), 16)
            @block.scalar
            def _(scalar, t0=t0, own=own):
                join_all(scalar)
                for n in range(NS):
                    SS.inc(scalar.dma_start(
                        out=s_Bb[:, n * TC:(n + 1) * TC],
                        in_=bc_d[n:n + 1, t0:t0 + TC].partition_broadcast(128)), 16)
                    if own:
                        SS.inc(scalar.dma_start(
                            out=s_Cb[:, n * TC:(n + 1) * TC],
                            in_=bc_d[16 + n:17 + n, t0:t0 + TC].partition_broadcast(128)), 16)
                if not own:
                    SS.inc(scalar.dma_start(
                        out=s_maskB,
                        in_=maskd[0:1, t0:t0 + TC].partition_broadcast(128)), 16)
            ld_w = LD.n
            ss_w = SS.n
            @block.vector
            def _(vector):
                vector.wait_ge(V.h, V.n)
            for dc in range(NKI):
                @block.vector
                def _(vector, dc=dc):
                    if dc == 0:
                        vector.wait_ge(LD.h, ld_w)
                        vector.wait_ge(SS.h, ss_w)
                    V.inc(vector.tensor_scalar(out=s_dt[dc], in0=s_dt[dc], scalar1=-80.0,
                                               scalar2=None, op0=ALU.max))
                    ctr[("spv", dc)] = V.n
                @block.scalar
                def _(scalar, dc=dc):
                    scalar.wait_ge(V.h, ctr[("spv", dc)])
                    if dc >= 1:
                        scalar.wait_ge(V.h, ctr[("spw", dc - 1)])
                    S.inc(scalar.activation(out=s_sp, in_=s_dt[dc], func=AF.Exp, scale=-1.0))
                    S.inc(scalar.activation(out=s_sp, in_=s_sp, func=AF.Ln, bias=1.0))
                    ctr[("spe", dc)] = S.n
                @block.vector
                def _(vector, dc=dc, own=own):
                    vector.wait_ge(S.h, ctr[("spe", dc)])
                    V.inc(vector.tensor_add(out=s_dt[dc], in0=s_dt[dc], in1=s_sp))
                    V.inc(vector.tensor_mul(out=s_u[dc], in0=s_dt[dc], in1=s_xcv[dc]))
                    if not own:
                        V.inc(vector.tensor_mul(out=s_u[dc], in0=s_u[dc], in1=s_maskB))
                    ctr[("spw", dc)] = V.n
            v_sp = V.n

            if not own:
                @block.vector
                def _(vector):
                    vector.memset(wst[:], 0.0)
                    for dc in range(NKI):
                        V.inc(vector.tensor_tensor_scan(out=s_scum[dc], data0=s_dt[dc],
                                                        data1=zerot16[:, 0:TC],
                                                        initial=scarry[:, dc:dc + 1],
                                                        op0=ALU.add, op1=ALU.add))
                    for dc in range(NKI):
                        V.inc(vector.tensor_copy(out=sendt[:, dc:dc + 1],
                                                 in_=s_scum[dc][:, TC - 1:TC]))
                    sc = V.n
                    vector.wait_ge(V.h, sc)
                    for dc in range(NKI):
                        V.inc(vector.tensor_scalar(out=s_scum[dc], in0=s_scum[dc],
                                                   scalar1=sendt[:, dc:dc + 1],
                                                   scalar2=None, op0=ALU.subtract))
                        ctr[("tsb", dc)] = V.n
                for dc in range(NKI):
                    for n in range(NS):
                        col = dc * NS + n
                        eb = s_E[n % 2]
                        ub = s_U[n % 2]
                        @block.scalar
                        def _(scalar, dc=dc, col=col, eb=eb):
                            scalar.wait_ge(V.h, ctr[("tsb", dc)])
                            if col >= 2:
                                scalar.wait_ge(V.h, ctr[("pacc", col - 2)])
                            S.inc(scalar.activation(out=eb, in_=s_scum[dc], func=AF.Exp,
                                                    scale=Apos[:, col:col + 1]))
                            ctr[("pE", col)] = S.n
                        @block.vector
                        def _(vector, dc=dc, n=n, col=col, eb=eb, ub=ub):
                            vector.wait_ge(S.h, ctr[("pE", col)])
                            V.inc(vector.tensor_mul(out=ub, in0=s_u[dc],
                                                    in1=s_Bb[:, n * TC:(n + 1) * TC]))
                            V.inc(vector.scalar_tensor_tensor(out=junk16[:, 0:TC], in0=eb,
                                                              scalar=1.0, in1=ub,
                                                              op0=ALU.mult, op1=ALU.mult,
                                                              accum_out=wst[:, col:col + 1]))
                            ctr[("pacc", col)] = V.n
                @block.vector
                def _(vector):
                    V.inc(vector.tensor_sub(out=dstt[:], in0=sendt[:], in1=scarry[:]))
                    V.inc(vector.tensor_copy(out=scarry[:], in_=sendt[:]))
                    du = V.n
                    vector.wait_ge(V.h, du)
                    for dc in range(NKI):
                        V.inc(vector.tensor_scalar(out=darg[:, dc * NS:(dc + 1) * NS],
                                                   in0=Aneg[:, dc * NS:(dc + 1) * NS],
                                                   scalar1=dstt[:, dc:dc + 1],
                                                   scalar2=None, op0=ALU.mult))
                v_da = V.n
                @block.scalar
                def _(scalar, v_da=v_da):
                    scalar.wait_ge(V.h, v_da)
                    S.inc(scalar.activation(out=darg[:], in_=darg[:], func=AF.Exp))
                s_da = S.n
                @block.vector
                def _(vector, s_da=s_da):
                    vector.wait_ge(S.h, s_da)
                    V.inc(vector.tensor_mul(out=stt[:], in0=stt[:], in1=darg[:]))
                    V.inc(vector.tensor_add(out=stt[:], in0=stt[:], in1=wst[:]))
            else:
                for dc in range(NKI):
                    ydc = s_y16[dc % 2]
                    ppy = psC if dc % 2 == 0 else psD
                    for n in range(NS):
                        col = dc * NS + n
                        g = col
                        da = s_dA[n % 2]
                        ub = s_U[n % 2]
                        hb = s_H[n % 2]
                        pb = s_P[n % 2]
                        @block.scalar
                        def _(scalar, dc=dc, col=col, g=g, da=da):
                            if g == 0:
                                scalar.wait_ge(V.h, v_sp)
                            if g >= 2:
                                scalar.wait_ge(V.h, ctr[("oscan", g - 2)])
                            S.inc(scalar.activation(out=da, in_=s_dt[dc], func=AF.Exp,
                                                    scale=Aneg[:, col:col + 1]))
                            ctr[("odA", g)] = S.n
                        @block.vector
                        def _(vector, dc=dc, n=n, col=col, g=g, da=da, ub=ub, hb=hb, pb=pb):
                            vector.wait_ge(S.h, ctr[("odA", g)])
                            if g >= 2:
                                vector.wait_ge(T.h, ctr[("oacc", g - 2)])
                            V.inc(vector.tensor_mul(out=ub, in0=s_u[dc],
                                                    in1=s_Bb[:, n * TC:(n + 1) * TC]))
                            V.inc(vector.tensor_tensor_scan(out=hb, data0=da, data1=ub,
                                                            initial=stt[:, col:col + 1],
                                                            op0=ALU.mult, op1=ALU.add))
                            V.inc(vector.tensor_copy(out=junk16[:, 0:8], in_=zerot16[:, 0:8]))
                            V.inc(vector.tensor_mul(out=pb, in0=hb,
                                                    in1=s_Cb[:, n * TC:(n + 1) * TC]))
                            V.inc(vector.tensor_copy(out=stt[:, col:col + 1],
                                                     in_=hb[:, TC - 1:TC]))
                            ctr[("oscan", g)] = V.n
                        @block.tensor
                        def _(tensor, dc=dc, n=n, g=g, pb=pb, ppy=ppy):
                            tensor.wait_ge(V.h, ctr[("oscan", g)])
                            if n == 0 and dc >= 2:
                                tensor.wait_ge(S.h, ctr[("oyev", dc - 2)])
                            T.inc(tensor.matmul(out=ppy[:, 0:TC], lhsT=idt16[:],
                                                rhs=pb, start=(n == 0), stop=False))
                            ctr[("oacc", g)] = T.n
                    dxb = s_dx[dc % 2]
                    @block.vector
                    def _(vector, dc=dc, dxb=dxb):
                        if dc >= 2:
                            vector.wait_ge(T.h, ctr[("oaccd", dc - 2)])
                        V.inc(vector.tensor_scalar(out=dxb, in0=s_xcv[dc],
                                                   scalar1=Dv[:, dc:dc + 1], scalar2=None,
                                                   op0=ALU.mult))
                        ctr[("odx", dc)] = V.n
                    @block.tensor
                    def _(tensor, dc=dc, ppy=ppy, dxb=dxb):
                        tensor.wait_ge(V.h, ctr[("odx", dc)])
                        T.inc(tensor.matmul(out=ppy[:, 0:TC], lhsT=idt16[:],
                                            rhs=dxb, start=False, stop=True))
                        ctr[("oaccd", dc)] = T.n
                    @block.scalar
                    def _(scalar, dc=dc, ydc=ydc, ppy=ppy):
                        scalar.wait_ge(T.h, ctr[("oaccd", dc)])
                        if dc >= 2:
                            scalar.wait_ge(SG.h, ctr[("oyst", dc - 2)])
                        S.inc(scalar.copy(out=ydc, in_=ppy[:, 0:TC]))
                        ctr[("oyev", dc)] = S.n
                    @block.gpsimd
                    def _(gpsimd, dc=dc, t0=t0, ydc=ydc):
                        gpsimd.wait_ge(SG.h, SG.n)
                        gpsimd.wait_ge(S.h, ctr[("oyev", dc)])
                        SG.inc(gpsimd.dma_start(
                            out=y_d[dc * 128:(dc + 1) * 128, t0 - OWN0:t0 - OWN0 + TC],
                            in_=ydc), 16)
                        ctr[("oyst", dc)] = SG.n

        @block.gpsimd
        def _(gpsimd):
            gpsimd.wait_ge(SG.h, SG.n)
            join_all(gpsimd)
            SG.inc(gpsimd.dma_start(out=dbg_d[:, 0:32], in_=bp[:]), 16)
            SG.inc(gpsimd.dma_start(out=dbg_d[:, 32:64], in_=bp2[:]), 16)
            SG.inc(gpsimd.dma_start(out=dbg_d[:, 64:72], in_=bfin[:]), 16)
        if stop_after == 'B':
            _emit_tail(block, SG)
            return _finish(es, nc)
        al = Alloc()
        wov = al.bf16(NKI * DM)
        c_y = [al.bf16(TC) for _ in range(NKI)]
        c_z = [al.bf16(TC) for _ in range(NKI)]
        c_yg = [al.bf16(TC) for _ in range(NKI)]
        c_sg = al.f32(TC)
        c_zg = al.f32(TC)
        c_x1 = [al.f32(TC) for _ in range(8)]
        c_stage = al.f32(4096)

        @block.sync
        def _(sync):
            join_all(sync)
        for k in range(NKI):
            @block.sync
            def _(sync, k=k):
                sync.wait_ge(LD.h, LD.n)
                if k >= 1:
                    sync.wait_ge(V.h, ctr[("woc", k - 1)])
                LD.inc(sync.dma_start(out=c_stage[:, 0:DM],
                                      in_=outwT[k * 128:(k + 1) * 128, :]), 16)
            ld_k = LD.n
            @block.vector
            def _(vector, k=k, ld_k=ld_k):
                vector.wait_ge(LD.h, ld_k)
                V.inc(vector.tensor_copy(out=wov[:, k * DM:(k + 1) * DM],
                                         in_=c_stage[:, 0:DM]))
                ctr[("woc", k)] = V.n

        for c in range(OWN0 // TC):
            t0 = c * TC
            @block.sync
            def _(sync, t0=t0):
                join_all(sync)
                for dc in range(NKI):
                    LD.inc(sync.dma_start(out=c_y[dc],
                                          in_=y_d[dc * 128:(dc + 1) * 128, t0:t0 + TC]), 16)
                    LD.inc(sync.dma_start(out=c_z[dc],
                                          in_=z_d[dc * 128:(dc + 1) * 128, t0:t0 + TC]), 16)
                for k in range(8):
                    LD.inc(sync.dma_start(
                        out=c_x1[k],
                        in_=x1_d[k * 128:(k + 1) * 128, OWN0 + t0:OWN0 + t0 + TC]), 16)
            ld_c = LD.n
            for dc in range(NKI):
                @block.scalar
                def _(scalar, dc=dc):
                    if dc == 0:
                        scalar.wait_ge(LD.h, ld_c)
                    if dc >= 1:
                        scalar.wait_ge(V.h, ctr[("cyg", dc - 1)])
                    S.inc(scalar.activation(out=c_sg, in_=c_z[dc], func=AF.Sigmoid))
                    ctr[("csg", dc)] = S.n
                @block.vector
                def _(vector, dc=dc):
                    if dc == 0:
                        vector.wait_ge(LD.h, ld_c)
                    vector.wait_ge(S.h, ctr[("csg", dc)])
                    V.inc(vector.tensor_mul(out=c_zg, in0=c_z[dc], in1=c_sg))
                    V.inc(vector.tensor_mul(out=c_yg[dc], in0=c_y[dc], in1=c_zg))
                    ctr[("cyg", dc)] = V.n
            v_g = V.n
            for mo in range(8):
                pp = psA if mo % 2 == 0 else psB
                @block.tensor
                def _(tensor, mo=mo, pp=pp):
                    if mo == 0:
                        tensor.wait_ge(V.h, v_g)
                    if mo >= 2:
                        tensor.wait_ge(V.h, ctr[("cx2", mo - 2)])
                    for k in range(NKI):
                        T.inc(tensor.matmul(out=pp[:, 0:TC],
                                            lhsT=wov[:, k * DM + mo * 128:k * DM + (mo + 1) * 128],
                                            rhs=c_yg[k], start=(k == 0), stop=(k == NKI - 1)))
                t_m = T.n
                @block.vector
                def _(vector, mo=mo, pp=pp, t_m=t_m):
                    vector.wait_ge(T.h, t_m)
                    V.inc(vector.tensor_add(out=c_x1[mo], in0=c_x1[mo], in1=pp[:, 0:TC]))
                    ctr[("cx2", mo)] = V.n
            v_x2 = V.n
            @block.gpsimd
            def _(gpsimd, t0=t0, v_x2=v_x2):
                gpsimd.wait_ge(SG.h, SG.n)
                gpsimd.wait_ge(V.h, v_x2)
                for k in range(8):
                    SG.inc(gpsimd.dma_start(out=x2_d[k * 128:(k + 1) * 128, t0:t0 + TC],
                                            in_=c_x1[k]), 16)

        if stop_after == 'C1':
            _emit_tail(block, SG)
            return _finish(es, nc)
        al = Alloc()
        w2v2 = al.bf16(NKD * DH)
        wcv2 = al.bf16(NKH * DM)
        d_x = [al.f32(TCF) for _ in range(8)]
        d_xn = [al.bf16(TCF) for _ in range(8)]
        d_c16 = [al.bf16(TCF) for _ in range(8)]
        d_sq = [al.bf16(TCF) for _ in range(8)]
        d_g2 = [al.bf16(TCF) for _ in range(NKH)]
        d_hb = [al.f32(TCF) for _ in range(2)]
        d_eb = [al.f32(TCF) for _ in range(2)]
        d_tmp = al.f32(TCF)
        d_o = [al.f32(TCF) for _ in range(8)]

        @block.sync
        def _(sync):
            join_all(sync)
            for k in range(NKD):
                LD.inc(sync.dma_start(out=w2v2[:, k * DH:(k + 1) * DH],
                                      in_=w2_d[k * 128:(k + 1) * 128, :]), 16)
            for k in range(NKH):
                LD.inc(sync.dma_start(out=wcv2[:, k * DM:(k + 1) * DM],
                                      in_=wc_d[k * 128:(k + 1) * 128, :]), 16)

        for c in range(OWN0 // TCF):
            t0 = c * TCF
            @block.sync
            def _(sync, t0=t0):
                join_all(sync)
                for k in range(8):
                    LD.inc(sync.dma_start(out=d_x[k],
                                          in_=x2_d[k * 128:(k + 1) * 128, t0:t0 + TCF]), 16)
            ld_c = LD.n
            @block.vector
            def _(vector):
                vector.wait_ge(LD.h, ld_c)
            @block.tensor
            def _(tensor):
                tensor.wait_ge(LD.h, ld_c)
            emit_ffn_chunk(w2v2, wcv2, d_x, d_xn, d_c16, d_sq, d_tmp, d_g2,
                           d_hb, d_eb, d_o, TCF)
            v_o = V.n
            @block.gpsimd
            def _(gpsimd, t0=t0, v_o=v_o):
                gpsimd.wait_ge(SG.h, SG.n)
                gpsimd.wait_ge(V.h, v_o)
                for k in range(8):
                    SG.inc(gpsimd.dma_start(out=out_d[k * 128:(k + 1) * 128, t0:t0 + TCF],
                                            in_=d_o[k]), 16)

        @block.gpsimd
        def _(gpsimd):
            gpsimd.wait_ge(SG.h, SG.n)
            gpsimd.wait_ge(SG.h, SG.n)

    return nc


def kernel(**inputs):
    nc = build()
    in_maps = [host_inputs(inputs, c // 2, c % 2) for c in range(8)]
    res = run_bass_kernel_spmd(nc, in_maps, core_ids=list(range(8)))
    out = np.empty((4, SLOTS, DM), np.float32)
    for c in range(8):
        b, s = c // 2, c % 2
        out[b, s * OWN0:(s + 1) * OWN0, :] = res.results[c]["out"].T
    return out
